# version 14 (final)
# speedup vs baseline: 1.6639x; 1.0154x over previous
import math
import numpy as np
from contextlib import ExitStack

import concourse.bass as bass
import concourse.mybir as mybir
from concourse.bass import IndirectOffsetOnAxis
from concourse.bass_utils import run_bass_kernel_spmd

F32 = mybir.dt.float32
BF16 = mybir.dt.bfloat16
I32 = mybir.dt.int32
U32 = mybir.dt.uint32
AF = mybir.ActivationFunctionType
ALU = mybir.AluOpType
AX = mybir.AxisListType

D = 1024
SEQ = 2048
NSEQ = 4
T = NSEQ * SEQ
NT = T // 128
E = 8
FF = 4096
H = 16
DH = 64
NHP = 8
ETILE = 512
NST = T // ETILE + (E - 1)
TPAD = (NST + 1) * ETILE
ROWB = 1032
LN_EPS = 1e-5

SUBLAYERS = 4
DEBUG = False

ENGINES = ("sync", "tensor", "vector", "scalar", "gpsimd")
ESEM = {"sync": "syE", "tensor": "tE", "vector": "vE", "scalar": "sE", "gpsimd": "gE"}
NDMASEM = 10
DSEMS = tuple(f"DS{k}" for k in range(NDMASEM)) + tuple(f"DG{k}" for k in range(NDMASEM))
ALLSEMS = ("syE", "tE", "vE", "sE", "gE") + DSEMS


def host_consts():
    c = 1.0 / math.sqrt(2.0)
    blk = np.zeros((8, 8), np.float64)
    blk[0, :] = c ** 3
    blk[1, :4] = c ** 3
    blk[1, 4:] = -(c ** 3)
    for k in range(2):
        blk[2 + k, 4 * k : 4 * k + 2] = c ** 2
        blk[2 + k, 4 * k + 2 : 4 * k + 4] = -(c ** 2)
    for k in range(4):
        blk[4 + k, 2 * k] = c
        blk[4 + k, 2 * k + 1] = -c
    A = np.zeros((128, 128), np.float32)
    for g in range(16):
        A[g * 8 : g * 8 + 8, g * 8 : g * 8 + 8] = blk.astype(np.float32)
    Pm = np.zeros((128, 128), np.float32)
    for i in range(64):
        Pm[2 * i, 2 * i + 1] = -1.0
        Pm[2 * i + 1, 2 * i] = 1.0
    return {"haarA": A, "haarAT": np.ascontiguousarray(A.T),
            "ropePT": np.ascontiguousarray(Pm.T),
            "ident_in": np.eye(128, dtype=np.float32)}


class Prog:
    def __init__(self):
        self.plan = {e: [] for e in ENGINES}
        self.counts = {s: 0 for s in ALLSEMS}
        self.know = {e: {s: 0 for s in ALLSEMS} for e in ENGINES}
        self.res = {}
        self.fencev = {s: 0 for s in ALLSEMS}
        self.dmaseq = {"DS": 0, "DG": 0}
        self.pend_w = []
        self.pend_r = []

    def fence(self):
        for s in ALLSEMS:
            self.fencev[s] = self.counts[s]

    def op(self, eng, fn, reads=(), writes=(), dma=False, accum=False, quiet=False):
        need = dict(self.fencev)
        for r in reads:
            e = self.res.get(r)
            if e:
                assert not (e.get("pend") and eng != "tensor"), f"pending-PE resource {r} read by {eng}"
                for (s, c) in e["w"]:
                    if c > need.get(s, 0):
                        need[s] = c
        for w in writes:
            e = self.res.get(w)
            if e:
                assert not (e.get("pend") and eng != "tensor"), f"pending-PE resource {w} written by {eng}"
                for (s, c) in e["w"]:
                    if c > need.get(s, 0):
                        need[s] = c
                for (s, c) in e["r"]:
                    if c > need.get(s, 0):
                        need[s] = c
        kn = self.know[eng]
        if dma:
            q = "DG" if eng == "gpsimd" else "DS"
            sem = f"{q}{self.dmaseq[q] % NDMASEM}"
            self.dmaseq[q] += 1
            if self.counts[sem] > need.get(sem, 0):
                need[sem] = self.counts[sem]
        else:
            sem = ESEM[eng]
        if eng == "tensor":
            need.pop("tE", None)
        waits = [(s, v) for s, v in need.items() if v > kn[s]]
        for s, v in waits:
            kn[s] = v
        if quiet:
            assert eng == "tensor" and not dma
            self.plan[eng].append((waits, fn, None, 0))
            for w in writes:
                e = self.res.setdefault(w, {"w": [], "r": []})
                if not accum:
                    e["w"] = []
                    e["r"] = []
                e["pend"] = True
                self.pend_w.append((w, accum))
            for r in reads:
                e = self.res.setdefault(r, {"w": [], "r": []})
                self.pend_r.append(r)
            return
        inc = 16 if dma else 1
        self.counts[sem] += inc
        cnt = self.counts[sem]
        self.plan[eng].append((waits, fn, sem, inc))
        if eng == "tensor" and (self.pend_w or self.pend_r):
            for (w, acc_) in self.pend_w:
                e = self.res.setdefault(w, {"w": [], "r": []})
                e["w"].append((sem, cnt))
                e["pend"] = False
            for r in self.pend_r:
                self.res.setdefault(r, {"w": [], "r": []})["r"].append((sem, cnt))
            self.pend_w = []
            self.pend_r = []
        for w in writes:
            e = self.res.setdefault(w, {"w": [], "r": []})
            if accum:
                e["w"].append((sem, cnt))
            else:
                e["w"] = [(sem, cnt)]
                e["r"] = []
        for r in reads:
            self.res.setdefault(r, {"w": [], "r": []})["r"].append((sem, cnt))


class Arena:
    def __init__(self, tensors):
        self.t = tensors
        self.off = {k: 0 for k in tensors}
        self.marks = []
        self.peak = {k: 0 for k in tensors}

    def alloc(self, dt, n):
        o = self.off[dt]
        self.off[dt] = o + n
        self.peak[dt] = max(self.peak[dt], self.off[dt])
        cap = self.t[dt].shape[1]
        assert self.off[dt] <= cap, f"arena {dt} overflow {self.off[dt]} > {cap}"
        return self.t[dt][:, o : o + n]

    def push(self):
        self.marks.append(dict(self.off))

    def pop(self):
        self.off = self.marks.pop()


def ts(i, n):
    return slice(i * n, (i + 1) * n)


def build_program(n_sub=None):
    if n_sub is None:
        n_sub = SUBLAYERS
    nc = bass.Bass("TRN2", target_bir_lowering=False, debug=False)

    din = lambda name, shape, dt=F32: nc.dram_tensor(name, shape, dt, kind="ExternalInput").ap()
    dint = lambda name, shape, dt: nc.dram_tensor(name, shape, dt, kind="Internal").ap()

    x_in = din("x_in", [T, D])
    wqvT_d = din("wqvT_d", [D, 2 * D])
    b_qkv = din("b_qkv", [1, 3 * D])
    waoT_d = din("waoT_d", [D, D])
    b_ao = din("b_ao", [1, D])
    wq_a = din("wq_a", [H, 3, DH])
    wv_a = din("wv_a", [H, 3, DH])
    wqT_d = din("wqT_d", [D, D])
    wkT_d = din("wkT_d", [D, D])
    wvT_d = din("wvT_d", [D, D])
    woT_d = din("woT_d", [D, D])
    w_r = din("w_r", [D, E])
    b_r = din("b_r", [1, E])
    ln_g = din("ln_g", [1, D])
    ln_b = din("ln_b", [1, D])
    w1f = din("w1f", [E * 8 * 128, FF])
    b1 = din("b1", [E, FF])
    w2f = din("w2f", [E * 4 * 128, 32 * 256])
    b2 = din("b2", [E, D])
    haarA_d = din("haarA", [128, 128])
    haarAT_d = din("haarAT", [128, 128])
    ropePT_d = din("ropePT", [128, 128])
    ident_d = din("ident_in", [128, 128])

    x_out = nc.dram_tensor("x_out", [T, D], F32, kind="ExternalOutput").ap()
    dbg = {}
    if DEBUG:
        for nm in ("d_xn", "d_multw"):
            dbg[nm] = nc.dram_tensor(nm, [128, 1024], F32, kind="ExternalOutput").ap()
        for nm in ("d_q", "d_v", "d_cq", "d_t1", "d_prod", "d_mix"):
            dbg[nm] = nc.dram_tensor(nm, [128, 1024], BF16, kind="ExternalOutput").ap()
        dbg["d_wqv"] = nc.dram_tensor("d_wqv", [D, 2 * D], BF16, kind="ExternalOutput").ap()
        dbg["d_hAT"] = nc.dram_tensor("d_hAT", [128, 128], BF16, kind="ExternalOutput").ap()
        for nm in ("d_xnbT", "d_qr", "d_kr", "d_ropeC", "d_ropeS"):
            dbg[nm] = nc.dram_tensor(nm, [128, 2048], BF16, kind="ExternalOutput").ap()
        dbg["d_vt"] = nc.dram_tensor("d_vt", [2048, 128], BF16, kind="ExternalOutput").ap()
        dbg["d_statc"] = nc.dram_tensor("d_statc", [128, 16], F32, kind="ExternalOutput").ap()
        dbg["d_statr"] = nc.dram_tensor("d_statr", [1, 2048], F32, kind="ExternalOutput").ap()
        dbg["d_probs"] = nc.dram_tensor("d_probs", [128, 512], BF16, kind="ExternalOutput").ap()
        dbg["d_pv"] = nc.dram_tensor("d_pv", [128, 512], BF16, kind="ExternalOutput").ap()

    RS = [x_in, dint("R1", [T, D], F32), dint("R2", [T, D], F32), dint("R3", [T, D], F32), None]
    RS[n_sub] = x_out

    w1p = dint("w1p", [E * 8 * 128, FF], BF16)
    w2p = dint("w2p", [E * 4 * 128, 32 * 256], BF16)
    b1p_d = dint("b1p_d", [E, FF], BF16)
    b2p_d = dint("b2p_d", [E, D], BF16)
    xn2nat = dint("xn2nat", [T, ROWB], BF16)
    xn2srt = dint("xn2srt", [TPAD, ROWB], BF16)
    ysrt = dint("ysrt", [TPAD, D], BF16)
    v_d = dint("v_d", [NSEQ * NHP * SEQ, 130], BF16)

    with ExitStack() as ctx:
        sbt = lambda n, s, d: ctx.enter_context(nc.sbuf_tensor(n, s, d))
        pst = lambda n, s, d: ctx.enter_context(nc.psum_tensor(n, s, d))

        ARB = sbt("ARB", [128, 63 * 1024], BF16)
        ARF = sbt("ARF", [128, 11 * 1024 + 256], F32)
        ident = sbt("ident", [128, 128], F32)
        hA = sbt("hA", [128, 128], BF16)
        hAT = sbt("hAT", [128, 128], BF16)
        rPT = sbt("rPT", [128, 128], BF16)
        ones_b = sbt("ones_b", [1, 128], BF16)
        ones_f = sbt("ones_f", [1, 128], F32)
        onesc_f = sbt("onesc_f", [128, 1], F32)
        ones_r512 = sbt("ones_r512", [1, 512], BF16)
        iota_p = sbt("iota_p", [128, 1], F32)
        iota_pi = sbt("iota_pi", [128, 1], I32)
        iota8 = sbt("iota8", [128, 8], F32)
        zeros8 = sbt("zeros8", [128, 8], F32)
        cmask = sbt("cmask", [128, 128], F32)
        multw = sbt("multw", [128, 1024], F32)
        ropeC = sbt("ropeC", [128, 2048], BF16)
        ropeS = sbt("ropeS", [128, 2048], BF16)
        g_cols = sbt("g_cols", [128, 8], F32)
        bt_colsf = sbt("bt_colsf", [128, 8], F32)
        wr_b = sbt("wr_b", [128, 64], BF16)
        br_row = sbt("br_row", [1, 8], F32)
        csum_b = sbt("csum_b", [128, 8], F32)
        br_b = sbt("br_b", [128, 8], F32)
        oh_all = sbt("oh_all", [128, 8 * NT], F32)
        gate_hi = sbt("gate_hi", [128, NT], F32)
        dest_i = sbt("dest_i", [128, NT], I32)
        carry8 = sbt("carry8", [8, 1], F32)
        cnt8 = sbt("cnt8", [8, 1], F32)
        padoffm1 = sbt("padoffm1", [8, 1], F32)
        ptilex = sbt("ptilex", [8, 1], F32)
        ltmask = sbt("ltmask", [8, 8], F32)
        idxw1 = sbt("idxw1", [128, 8 * NST], I32)
        idxw2 = sbt("idxw2", [128, 4 * NST], I32)
        idxbr = sbt("idxbr", [2, NST], I32)
        iota23 = sbt("iota23", [1, NST], F32)
        scanbuf = sbt("scanbuf", [8, 512], F32)
        scanone = sbt("scanone", [8, 512], F32)
        ohT = sbt("ohT", [8, 512], F32)
        tmpT8 = sbt("tmpT8", [8, 512], F32)
        drow = sbt("drow", [1, 512], F32)
        erow_m1 = sbt("erow_m1", [1, NST], F32)
        ecmp = sbt("ecmp", [8, NST], F32)
        ln_mu = [sbt(f"ln_mu{r}", [128, 1], F32) for r in range(4)]
        ln_ss = [sbt(f"ln_ss{r}", [128, 1], F32) for r in range(4)]
        ln_var = [sbt(f"ln_var{r}", [128, 1], F32) for r in range(4)]
        ln_rs = [sbt(f"ln_rs{r}", [128, 1], F32) for r in range(4)]
        ln_sq = [sbt(f"ln_sq{r}", [128, 1024], F32) for r in range(1)]
        miu = [sbt(f"miu{r}", [128, 8], U32) for r in range(2)]
        i32b = sbt("i32b", [8, 1], I32)

        pA = pst("pA", [128, 1024], F32)
        pB = pst("pB", [128, 1024], F32)
        pC = pst("pC", [128, 1024], F32)
        pD = pst("pD", [128, 1024], F32)
        PS = {"A": pA, "B": pB, "C": pC, "D": pD}

        semh = {s: ctx.enter_context(nc.semaphore(s)) for s in ALLSEMS}
        block = ctx.enter_context(nc.Block())

        P = Prog()
        AR = Arena({"b": ARB, "f": ARF})

        def bank(n, half):
            return PS[n][:, half * 512 : (half + 1) * 512], f"p{n}{half}"

        def qbank(n, q):
            return PS[n][:, q * 256 : (q + 1) * 256], f"p{n}{q // 2}"

        def sub128(n, j):
            return PS[n][:, j * 128 : (j + 1) * 128], f"p{n}{j // 4}"

        def init_consts():
            P.op("sync", lambda e: e.dma_start(out=ident[:], in_=ident_d[:, :]), writes=["ident"], dma=True)
            for buf, key, val in ((ones_b, "ones_b", 1.0), (ones_f, "ones_f", 1.0),
                                  (onesc_f, "onesc_f", 1.0), (ones_r512, "ones_r512", 1.0),
                                  (zeros8, "zeros8", 0.0), (scanone, "scanone", 1.0)):
                P.op("vector", lambda e, b=buf, v=val: e.memset(b[:], v), writes=[key])
            tmpf = AR.alloc("f", 128)
            for (dst, key, src) in ((hA, "hA", haarA_d), (hAT, "hAT", haarAT_d), (rPT, "rPT", ropePT_d)):
                P.op("sync", lambda e, s=src: e.dma_start(out=tmpf[:], in_=s[:, :]), writes=["tmpf"], dma=True)
                P.op("vector", lambda e, d=dst: e.tensor_copy(d[:], tmpf[:]), reads=["tmpf"], writes=[key])
            P.op("gpsimd", lambda e: e.iota(iota_pi[:], pattern=[[0, 1]], base=0, channel_multiplier=1),
                 writes=["iota_pi"])
            P.op("vector", lambda e: e.tensor_copy(iota_p[:], iota_pi[:]), reads=["iota_pi"], writes=["iota_p"])
            P.op("gpsimd", lambda e: e.iota(iota8[:], pattern=[[1, 8]], base=0, channel_multiplier=0,
                                            allow_small_or_imprecise_dtypes=True), writes=["iota8"])
            P.op("gpsimd", lambda e: e.iota(iota23[:], pattern=[[1, NST]], base=0, channel_multiplier=0,
                                            allow_small_or_imprecise_dtypes=True), writes=["iota23"])
            P.op("vector", lambda e: e.memset(cmask[:], 0.0), writes=["cmask"])
            P.op("gpsimd", lambda e: e.affine_select(cmask[:], cmask[:], pattern=[[-1, 128]],
                                                     compare_op=ALU.is_ge, fill=-1e30,
                                                     base=0, channel_multiplier=1),
                 reads=["cmask"], writes=["cmask"])
            P.op("vector", lambda e: e.memset(ltmask[:], 1.0), writes=["ltmask"])
            P.op("gpsimd", lambda e: e.affine_select(ltmask[:], ltmask[:], pattern=[[1, 8]],
                                                     compare_op=ALU.is_gt, fill=0.0,
                                                     base=0, channel_multiplier=-1),
                 reads=["ltmask"], writes=["ltmask"])
            for j in range(8):
                P.op("sync", lambda e, j=j: e.dma_start(
                    out=g_cols[:, j : j + 1],
                    in_=ln_g.rearrange("o (j p) -> j p o", j=8, p=128)[j]),
                     writes=["g_cols"], dma=True, accum=True)
            tmpb = AR.alloc("f", 8)
            for j in range(8):
                P.op("sync", lambda e, j=j: e.dma_start(
                    out=tmpb[:, j : j + 1],
                    in_=ln_b.rearrange("o (j p) -> j p o", j=8, p=128)[j]),
                     writes=["tmpb"], dma=True, accum=True)
            P.op("vector", lambda e: e.tensor_copy(bt_colsf[:], tmpb[:]), reads=["tmpb"], writes=["bt_colsf"])
            tmpwr = AR.alloc("f", 64)
            for j in range(8):
                P.op("sync", lambda e, j=j: e.dma_start(
                    out=tmpwr[:, ts(j, 8)], in_=w_r[ts(j, 128), :]), writes=["tmpwr"], dma=True, accum=True)
            P.op("vector", lambda e: e.tensor_copy(wr_b[:], tmpwr[:]), reads=["tmpwr"], writes=["wr_b"])
            P.op("sync", lambda e: e.dma_start(out=br_row[:], in_=b_r[:, :]), writes=["br_row"], dma=True)
            ones128 = AR.alloc("b", 128)
            P.op("vector", lambda e: e.memset(ones128[:], 1.0), writes=["ones128"])
            for j in range(8):
                P.op("tensor", lambda e, j=j: e.matmul(pA[:, 0:8], ones128[:], wr_b[:, ts(j, 8)],
                                                       start=(j == 0), stop=(j == 7)),
                     reads=["ones128", "wr_b"], writes=["pA0"], quiet=(j < 7))
            P.op("vector", lambda e: e.tensor_copy(csum_b[:], pA[:, 0:8]), reads=["pA0"], writes=["csum_b"])
            P.op("tensor", lambda e: e.matmul(pA[:, 0:8], ones_f[:], br_row[:], start=True, stop=True),
                 reads=["ones_f", "br_row"], writes=["pA0"])
            P.op("vector", lambda e: e.tensor_copy(br_b[:], pA[:, 0:8]), reads=["pA0"], writes=["br_b"])

            AR.push()
            wqr = AR.alloc("f", 3 * 1024)
            wvr = AR.alloc("f", 3 * 1024)
            for (buf, key, src) in ((wqr, "wqr", wq_a), (wvr, "wvr", wv_a)):
                for j in range(3):
                    P.op("sync", lambda e, j=j, b=buf, s=src: e.dma_start(
                        out=b[0:1, ts(j, 1024)],
                        in_=s.rearrange("h j d -> j h d")[j]),
                         writes=[key], dma=True, accum=True)
            pm8i = sbt("pm8i", [128, 1], I32)
            pm8 = AR.alloc("f", 1)
            P.op("vector", lambda e: e.tensor_single_scalar(pm8i[:], iota_pi[:], 7, op=ALU.bitwise_and),
                 reads=["iota_pi"], writes=["pm8i"])
            P.op("vector", lambda e: e.tensor_copy(pm8[:], pm8i[:]), reads=["pm8i"], writes=["pm8"])
            msk = [AR.alloc("f", 1) for _ in range(4)]
            P.op("vector", lambda e: e.tensor_single_scalar(msk[0][:], pm8[:], 0.0, op=ALU.is_equal),
                 reads=["pm8"], writes=["msk0"])
            P.op("vector", lambda e: e.tensor_single_scalar(msk[1][:], pm8[:], 1.0, op=ALU.is_equal),
                 reads=["pm8"], writes=["msk1"])
            t2 = AR.alloc("f", 1)
            P.op("vector", lambda e: e.tensor_single_scalar(msk[2][:], pm8[:], 2.0, op=ALU.is_ge),
                 reads=["pm8"], writes=["msk2"])
            P.op("vector", lambda e: e.tensor_single_scalar(t2[:], pm8[:], 3.0, op=ALU.is_le),
                 reads=["pm8"], writes=["t2"])
            P.op("vector", lambda e: e.tensor_mul(msk[2][:], msk[2][:], t2[:]), reads=["msk2", "t2"], writes=["msk2"])
            P.op("vector", lambda e: e.memset(msk[3][:], 1.0), writes=["msk3"])
            for k in range(3):
                P.op("vector", lambda e, k=k: e.tensor_sub(msk[3][:], msk[3][:], msk[k][:]),
                     reads=["msk3", f"msk{k}"], writes=["msk3"])
            wb = AR.alloc("f", 1024)
            P.op("vector", lambda e: e.memset(multw[:], 0.0), writes=["multw"])
            for j in range(3):
                for hh in range(2):
                    P.op("tensor", lambda e, j=j, hh=hh: e.matmul(
                        pA[:, ts(hh, 512)], ones_f[:], wqr[0:1, j * 1024 + hh * 512 : j * 1024 + (hh + 1) * 512],
                        start=True, stop=True),
                         reads=["ones_f", "wqr"], writes=[f"pA{hh}"])
                P.op("vector", lambda e: e.tensor_copy(wb[:], pA[:, 0:1024]), reads=["pA0", "pA1"], writes=["wb"])
                for hh in range(2):
                    P.op("tensor", lambda e, j=j, hh=hh: e.matmul(
                        pA[:, ts(hh, 512)], ones_f[:], wvr[0:1, j * 1024 + hh * 512 : j * 1024 + (hh + 1) * 512],
                        start=True, stop=True),
                         reads=["ones_f", "wvr", "wb"], writes=[f"pA{hh}"])
                P.op("vector", lambda e: e.tensor_mul(wb[:], wb[:], pA[:, 0:1024]),
                     reads=["wb", "pA0", "pA1"], writes=["wb"])
                P.op("vector", lambda e, j=j: e.tensor_scalar(wb[:], wb[:], msk[j][:], None, op0=ALU.mult),
                     reads=["wb", f"msk{j}"], writes=["wb"])
                P.op("vector", lambda e: e.tensor_add(multw[:], multw[:], wb[:]), reads=["multw", "wb"], writes=["multw"])
            P.op("vector", lambda e: e.tensor_scalar(multw[:], multw[:], msk[3][:], None, op0=ALU.add),
                 reads=["multw", "msk3"], writes=["multw"])
            P.fence()
            AR.pop()

            u = AR.alloc("f", 1)
            ui = sbt("ui", [128, 1], I32)
            P.op("vector", lambda e: e.tensor_single_scalar(ui[:], iota_pi[:], 63, op=ALU.bitwise_and),
                 reads=["iota_pi"], writes=["ui"])
            P.op("vector", lambda e: e.tensor_single_scalar(ui[:], ui[:], 1, op=ALU.arith_shift_right),
                 reads=["ui"], writes=["ui"])
            P.op("vector", lambda e: e.tensor_copy(u[:], ui[:]), reads=["ui"], writes=["u"])
            invf = AR.alloc("f", 1)
            P.op("scalar", lambda e: e.activation(invf[:], u[:], AF.Exp, bias=0.0,
                                                  scale=-math.log(10000.0) / 32.0),
                 reads=["u"], writes=["invf"])
            tio = AR.alloc("f", 2048)
            P.op("gpsimd", lambda e: e.iota(tio[:], pattern=[[1, 2048]], base=0, channel_multiplier=0,
                                            allow_small_or_imprecise_dtypes=True), writes=["tio"])
            ph = AR.alloc("f", 2048)
            fl = AR.alloc("f", 2048)
            co = AR.alloc("f", 2048)
            fli = sbt("fli", [128, 256], I32)
            for tab, key, shift in ((ropeS, "ropeS", 0.0), (ropeC, "ropeC", math.pi / 2.0)):
                P.op("vector", lambda e: e.tensor_scalar(ph[:], tio[:], invf[:], None, op0=ALU.mult),
                     reads=["tio", "invf"], writes=["ph"])
                if shift:
                    P.op("vector", lambda e, s=shift: e.tensor_scalar_add(ph[:], ph[:], s), reads=["ph"], writes=["ph"])
                P.op("vector", lambda e: e.tensor_scalar_mul(ph[:], ph[:], 1.0 / (2 * math.pi)),
                     reads=["ph"], writes=["ph"])
                for ch in range(8):
                    P.op("vector", lambda e, ch=ch: e.tensor_copy(fli[:], ph[:, ts(ch, 256)]),
                         reads=["ph"], writes=["fli"])
                    P.op("vector", lambda e, ch=ch: e.tensor_copy(fl[:, ts(ch, 256)], fli[:]),
                         reads=["fli"], writes=["fl"], accum=(ch > 0))
                P.op("vector", lambda e: e.tensor_tensor(co[:], fl[:], ph[:], op=ALU.is_gt),
                     reads=["fl", "ph"], writes=["co"])
                P.op("vector", lambda e: e.tensor_sub(fl[:], fl[:], co[:]), reads=["fl", "co"], writes=["fl"])
                P.op("vector", lambda e: e.tensor_sub(ph[:], ph[:], fl[:]), reads=["ph", "fl"], writes=["ph"])
                P.op("vector", lambda e: e.tensor_scalar(ph[:], ph[:], 2 * math.pi, -math.pi,
                                                         op0=ALU.mult, op1=ALU.add),
                     reads=["ph"], writes=["ph"])
                P.op("scalar", lambda e, t=tab: e.activation(t[:], ph[:], AF.Sin, bias=0.0, scale=1.0),
                     reads=["ph"], writes=[key])
                P.op("vector", lambda e, t=tab, k=key: e.tensor_scalar_mul(t[:], t[:], -1.0),
                     reads=[key], writes=[key])

        def make_recast_gen():
            wbb = [AR.alloc("b", 4096) for _ in range(2)]

            def gen():
                for e_ in range(E):
                    r = e_ % 2
                    P.op("gpsimd", lambda e, e_=e_, b=wbb[r]: e.dma_start(
                        out=b[0:1, :], in_=b1[e_ : e_ + 1, :]), writes=[f"rc_w{r}"], dma=True)
                    P.op("sync", lambda e, e_=e_, b=wbb[r]: e.dma_start(
                        out=b1p_d[e_ : e_ + 1, :], in_=b[0:1, :]),
                         reads=[f"rc_w{r}"], writes=["b1p_dd"], dma=True, accum=True)
                    P.op("gpsimd", lambda e, e_=e_, b=wbb[r]: e.dma_start(
                        out=b[0:1, 0:D], in_=b2[e_ : e_ + 1, :]), writes=[f"rc_w{r}"], dma=True)
                    P.op("sync", lambda e, e_=e_, b=wbb[r]: e.dma_start(
                        out=b2p_d[e_ : e_ + 1, :], in_=b[0:1, 0:D]),
                         reads=[f"rc_w{r}"], writes=["b2p_dd"], dma=True, accum=True)
                yield
                for k in range(64):
                    r = k % 2
                    P.op("gpsimd", lambda e, k=k, b=wbb[r]: e.dma_start(
                        out=b[:], in_=w1f[ts(k, 128), :]), writes=[f"rc_w{r}"], dma=True)
                    P.op("sync", lambda e, k=k, b=wbb[r]: e.dma_start(
                        out=w1p[ts(k, 128), :], in_=b[:]),
                         reads=[f"rc_w{r}"], writes=["w1p_d"], dma=True, accum=True)
                    yield
                for k in range(64):
                    r = k % 2
                    row, half = k // 2, k % 2
                    P.op("gpsimd", lambda e, row=row, half=half, b=wbb[r]: e.dma_start(
                        out=b[:], in_=w2f[ts(row, 128), half * 4096 : (half + 1) * 4096]),
                         writes=[f"rc_w{r}"], dma=True)
                    P.op("sync", lambda e, row=row, half=half, b=wbb[r]: e.dma_start(
                        out=w2p[ts(row, 128), half * 4096 : (half + 1) * 4096], in_=b[:]),
                         reads=[f"rc_w{r}"], writes=["w2p_d"], dma=True, accum=True)
                    yield
            return gen()

        def emit_ln(x_ap, xkey, out_ap, outkey, r, out_accum=False):
            mu, ssb, var, rs = ln_mu[r], ln_ss[r], ln_var[r], ln_rs[r]
            sq = ln_sq[0]
            kmu, kss, kvar, krs, ksq = (f"ln_mu{r}", f"ln_ss{r}", f"ln_var{r}", f"ln_rs{r}", "ln_sq0")
            P.op("vector", lambda e: e.reduce_sum(mu[:], x_ap, axis=AX.X), reads=[xkey], writes=[kmu])
            P.op("vector", lambda e: e.tensor_scalar_mul(mu[:], mu[:], 1.0 / 1024.0), reads=[kmu], writes=[kmu])
            P.op("scalar", lambda e: e.activation(sq[:], x_ap, AF.Square, bias=0.0, scale=1.0,
                                                  accum_out=ssb[:]),
                 reads=[xkey], writes=[ksq, kss])
            P.op("vector", lambda e: e.tensor_mul(var[:], mu[:], mu[:]), reads=[kmu], writes=[kvar])
            P.op("vector", lambda e: e.scalar_tensor_tensor(var[:], ssb[:], 1.0 / 1024.0, var[:],
                                                            op0=ALU.mult, op1=ALU.subtract),
                 reads=[kss, kvar], writes=[kvar])
            P.op("vector", lambda e: e.tensor_scalar_add(var[:], var[:], LN_EPS), reads=[kvar], writes=[kvar])
            P.op("scalar", lambda e: e.activation(rs[:], var[:], AF.Sqrt, bias=0.0, scale=1.0),
                 reads=[kvar], writes=[krs])
            P.op("vector", lambda e: e.reciprocal(rs[:], rs[:]), reads=[krs], writes=[krs])
            if out_ap is not None:
                P.op("vector", lambda e: e.tensor_scalar(out_ap, x_ap, mu[:], rs[:],
                                                         op0=ALU.subtract, op1=ALU.mult),
                     reads=[xkey, kmu, krs], writes=[outkey], accum=out_accum)

        def attn_a(Rin, Rout, kin, kout, bg=None):
            AR.push()
            wqv_sb = AR.alloc("b", 8 * 2048)
            wao_sb = AR.alloc("b", 8 * 1024)
            P.op("gpsimd", lambda e: e.dma_start(
                out=wqv_sb, in_=wqvT_d.rearrange("(j p) c -> p j c", j=8, p=128)),
                 writes=["wqv_sb"], dma=True)
            P.op("gpsimd", lambda e: e.dma_start(
                out=wao_sb, in_=waoT_d.rearrange("(j p) c -> p j c", j=8, p=128)),
                 writes=["wao_sb"], dma=True)
            bqkv_b = AR.alloc("b", 3 * D)[0:1, :]
            bao_b = AR.alloc("b", D)[0:1, :]
            tst = AR.alloc("f", 3 * D)
            P.op("sync", lambda e: e.dma_start(out=tst[0:1, :], in_=b_qkv[:, :]), writes=["tst"], dma=True)
            P.op("vector", lambda e: e.tensor_copy(bqkv_b[:], tst[0:1, :]), reads=["tst"], writes=["bqkv_b"])
            P.op("sync", lambda e: e.dma_start(out=tst[0:1, 0:D], in_=b_ao[:, :]), writes=["tst"], dma=True)
            P.op("vector", lambda e: e.tensor_copy(bao_b[:], tst[0:1, 0:D]), reads=["tst"], writes=["bao_b"])
            rl = [AR.alloc("f", 1024) for _ in range(3)]
            xn = [AR.alloc("f", 1024) for _ in range(2)]
            xnT = [AR.alloc("b", 1024) for _ in range(2)]
            qb = [AR.alloc("b", 1024) for _ in range(2)]
            vb = [AR.alloc("b", 1024) for _ in range(2)]
            cqb = [AR.alloc("b", 1024) for _ in range(2)]
            t1b = [AR.alloc("b", 1024) for _ in range(2)]
            prodb = [AR.alloc("b", 1024) for _ in range(2)]
            mixb = [AR.alloc("b", 1024) for _ in range(2)]
            ro = [AR.alloc("f", 1024) for _ in range(2)]
            def a_tile(i):
                r3, r2 = i % 3, i % 2
                MMB = "A" if i % 2 == 0 else "B"
                AUX = "C" if i % 2 == 0 else "D"
                krl, kxn, kxnT = f"a_rl{r3}", f"a_xn{r2}", f"a_xnT{r2}"
                kq, kv, kcq, kt1, kpr, kmx, kro = (f"a_q{r2}", f"a_v{r2}", f"a_cq{r2}",
                                                   f"a_t1{r2}", f"a_pr{r2}", f"a_mx{r2}", f"a_ro{r2}")
                rlc, xnc, xnTc = rl[r3], xn[r2], xnT[r2]
                qbc, vbc, cqc, t1c, prc, mxc, roc = (qb[r2], vb[r2], cqb[r2], t1b[r2],
                                                     prodb[r2], mixb[r2], ro[r2])
                P.op("sync", lambda e, i=i, b=rlc: e.dma_start(out=b[:], in_=Rin[ts(i, 128), :]),
                     reads=[f"{kin}:{i}"] if kin else [], writes=[krl], dma=True)
                emit_ln(rlc[:], krl, xnc[:], kxn, i % 4)
                for j in range(8):
                    pb, pk = sub128(AUX, j)
                    P.op("tensor", lambda e, j=j, p=pb, x=xnc: e.transpose(p[:], x[:, ts(j, 128)], ident[:]),
                         reads=[kxn, "ident"], writes=[pk])
                    P.op("scalar", lambda e, j=j, p=pb, x=xnTc: e.copy(x[:, ts(j, 128)], p[:]),
                         reads=[pk], writes=[kxnT], accum=(j > 0))
                for c in range(4):
                    pb, pk = bank(MMB, c % 2)
                    for j in range(8):
                        P.op("tensor", lambda e, c=c, j=j, p=pb, x=xnTc: e.matmul(
                            p[:], x[:, ts(j, 128)],
                            wqv_sb[:, j * 2048 + c * 512 : j * 2048 + (c + 1) * 512],
                            start=(j == 0), stop=False),
                             reads=[kxnT, "wqv_sb"], writes=[pk], quiet=True)
                    bc = c * 512 if c < 2 else 2 * D + (c - 2) * 512
                    P.op("tensor", lambda e, bc=bc, p=pb: e.matmul(
                        p[:], ones_b[:], bqkv_b[:, bc : bc + 512], start=False, stop=True),
                         reads=["ones_b", "bqkv_b"], writes=[pk])
                    dstb, dk = (qbc, kq) if c < 2 else (vbc, kv)
                    P.op("scalar", lambda e, c=c, p=pb, d=dstb: e.copy(d[:, ts(c % 2, 512)], p[:]),
                         reads=[pk], writes=[dk], accum=(c % 2 > 0))
                for c in range(2):
                    pb, pk = bank(AUX, c)
                    P.op("tensor", lambda e, c=c, p=pb: e.matmul(p[:], hAT[:], qbc[:, ts(c, 512)],
                                                                 start=True, stop=True),
                         reads=["hAT", kq], writes=[pk])
                    P.op("scalar", lambda e, c=c, p=pb: e.copy(cqc[:, ts(c, 512)], p[:]),
                         reads=[pk], writes=[kcq], accum=(c > 0))
                for c in range(2):
                    pb, pk = bank(AUX, c)
                    P.op("tensor", lambda e, c=c, p=pb: e.matmul(p[:], hAT[:], vbc[:, ts(c, 512)],
                                                                 start=True, stop=True),
                         reads=["hAT", kv], writes=[pk])
                    P.op("vector", lambda e, c=c, p=pb: e.tensor_tensor(
                        t1c[:, ts(c, 512)], p[:], multw[:, ts(c, 512)], op=ALU.mult),
                         reads=[pk, "multw"], writes=[kt1], accum=(c > 0))
                P.op("vector", lambda e: e.tensor_mul(prc[:], t1c[:], cqc[:]),
                     reads=[kt1, kcq], writes=[kpr])
                for j in range(8):
                    pb, pk = sub128(AUX, j)
                    P.op("tensor", lambda e, j=j, p=pb: e.matmul(p[:], prc[:, ts(j, 128)], hA[:],
                                                                 start=True, stop=True),
                         reads=[kpr, "hA"], writes=[pk])
                    P.op("scalar", lambda e, j=j, p=pb: e.copy(mxc[:, ts(j, 128)], p[:]),
                         reads=[pk], writes=[kmx], accum=(j > 0))
                for c in range(2):
                    pb, pk = bank(MMB, c)
                    for j in range(8):
                        P.op("tensor", lambda e, c=c, j=j, p=pb: e.matmul(
                            p[:], mxc[:, ts(j, 128)],
                            wao_sb[:, j * 1024 + c * 512 : j * 1024 + (c + 1) * 512],
                            start=(j == 0), stop=False),
                             reads=[kmx, "wao_sb"], writes=[pk], quiet=True)
                    P.op("tensor", lambda e, c=c, p=pb: e.matmul(
                        p[:], ones_b[:], bao_b[:, ts(c, 512)], start=False, stop=True),
                         reads=["ones_b", "bao_b"], writes=[pk])
                    P.op("vector", lambda e, c=c, p=pb: e.tensor_add(
                        roc[:, ts(c, 512)], p[:], rlc[:, ts(c, 512)]),
                         reads=[pk, krl], writes=[kro], accum=(c > 0))
                P.op("sync", lambda e, i=i: e.dma_start(out=Rout[ts(i, 128), :], in_=roc[:]),
                     reads=[kro], writes=[f"{kout}:{i}"], dma=True)
                if DEBUG and i == 0:
                    P.op("sync", lambda e: e.dma_start(out=dbg["d_wqv"][:, :], in_=wqvT_d[:, :]),
                         writes=["dbgwqv"], dma=True)
                    P.op("sync", lambda e: e.dma_start(out=dbg["d_hAT"][:, :], in_=hAT[:]),
                         reads=["hAT"], writes=["dbghat"], dma=True)
                    for nm, buf, key in (("d_xn", xnc, kxn), ("d_q", qbc, kq),
                                         ("d_v", vbc, kv), ("d_cq", cqc, kcq),
                                         ("d_t1", t1c, kt1), ("d_prod", prc, kpr),
                                         ("d_mix", mxc, kmx), ("d_multw", multw, "multw")):
                        P.op("sync", lambda e, nm=nm, b=buf: e.dma_start(out=dbg[nm][:, :], in_=b[:]),
                             reads=[key], writes=[f"dbg{nm}"], dma=True)

            for i in range(NT):
                a_tile(i)
                if bg is not None:
                    for _ in range(3):
                        next(bg, None)
            AR.pop()
            P.fence()

        def ffn(app, Rin, Rout, kin, kout):
            AR.push()
            AR.push()
            rl = [AR.alloc("f", 1024) for _ in range(3)]
            xtT = [AR.alloc("b", 1024) for _ in range(2)]
            zg = [AR.alloc("b", ROWB) for _ in range(3)]
            rr2 = [AR.alloc("f", 1) for _ in range(4)]
            mur = [AR.alloc("f", 1) for _ in range(2)]
            tm8 = [AR.alloc("f", 8) for _ in range(2)]
            lgs = [AR.alloc("f", 8) for _ in range(2)]
            mx8 = [AR.alloc("f", 1) for _ in range(2)]
            mw8 = [AR.alloc("f", 8) for _ in range(2)]
            se8 = [AR.alloc("f", 8) for _ in range(2)]
            ssum = [AR.alloc("f", 1) for _ in range(2)]
            gcol = [AR.alloc("f", 1) for _ in range(2)]
            glo = [AR.alloc("f", 1) for _ in range(2)]
            ecol = [AR.alloc("f", 1) for _ in range(2)]
            def f_tile(i):
                r3, r2, r4 = i % 3, i % 2, i % 4
                krl, kxtT, kzg = f"f_rl{r3}", f"f_xtT{r2}", f"f_zg{r3}"
                rlc, xtTc, zgc = rl[r3], xtT[r2], zg[r3]
                AUX = "C" if i % 2 == 0 else "D"
                lgc, mxc, mwc, sec, ssc, gcc, glc, ecc, miuc = (lgs[r2], mx8[r2], mw8[r2], se8[r2],
                                                                ssum[r2], gcol[r2], glo[r2], ecol[r2], miu[r2])
                P.op("sync", lambda e, i=i, b=rlc: e.dma_start(out=b[:], in_=Rin[ts(i, 128), :]),
                     reads=[f"{kin}:{i}"] if kin else [], writes=[krl], dma=True)
                emit_ln(rlc[:], krl, None, None, r4)
                mu, rs = ln_mu[r4], ln_rs[r4]
                kmu, kvar, krs = f"ln_mu{r4}", f"ln_var{r4}", f"ln_rs{r4}"
                rs2c, krs2 = rr2[r4], f"f_rs2{r4}"
                P.op("vector", lambda e, c=rs2c: e.tensor_scalar(
                    c[:], ln_var[r4][:], 1.0 + LN_EPS, -LN_EPS, op0=ALU.mult, op1=ALU.add),
                     reads=[kvar], writes=[krs2])
                P.op("scalar", lambda e, c=rs2c: e.activation(c[:], c[:], AF.Sqrt, bias=0.0, scale=1.0),
                     reads=[krs2], writes=[krs2])
                P.op("vector", lambda e, c=rs2c: e.reciprocal(c[:], c[:]), reads=[krs2], writes=[krs2])
                P.op("vector", lambda e, c=rs2c: e.tensor_scalar(
                    zgc[:, 0:1024], rlc[:], mu[:], c[:], op0=ALU.subtract, op1=ALU.mult),
                     reads=[krl, kmu, krs2], writes=[kzg])
                for j in range(8):
                    pb, pk = sub128(AUX, j)
                    P.op("tensor", lambda e, j=j, p=pb: e.transpose(p[:], rlc[:, ts(j, 128)], ident[:]),
                         reads=[krl, "ident"], writes=[pk])
                    P.op("scalar", lambda e, j=j, p=pb: e.copy(xtTc[:, ts(j, 128)], p[:]),
                         reads=[pk], writes=[kxtT], accum=(j > 0))
                pb, pk = bank("A", i % 2)
                for j in range(8):
                    P.op("tensor", lambda e, j=j, p=pb: e.matmul(
                        p[:, 0:8], xtTc[:, ts(j, 128)], wr_b[:, ts(j, 8)],
                        start=(j == 0), stop=(j == 7)),
                         reads=[kxtT, "wr_b"], writes=[pk], quiet=(j < 7))
                klg, kmx, kse, kg, ke = f"f_lg{r2}", f"f_mx{r2}", f"f_se{r2}", f"f_g{r2}", f"f_e{r2}"
                kmur, ktm8 = f"f_mur{r2}", f"f_tm8{r2}"
                P.op("vector", lambda e, p=pb: e.tensor_scalar(lgc[:], p[:, 0:8], rs[:], None,
                                                               op0=ALU.mult),
                     reads=[pk, krs], writes=[klg])
                P.op("vector", lambda e, c=mur[r2]: e.tensor_scalar(c[:], mu[:], rs[:], -1.0,
                                                                    op0=ALU.mult, op1=ALU.mult),
                     reads=[kmu, krs], writes=[kmur])
                P.op("vector", lambda e, c=tm8[r2], m=mur[r2]: e.tensor_scalar(
                    c[:], csum_b[:], m[:], None, op0=ALU.mult),
                     reads=["csum_b", kmur], writes=[ktm8])
                P.op("vector", lambda e, c=tm8[r2]: e.tensor_add(lgc[:], lgc[:], c[:]),
                     reads=[klg, ktm8], writes=[klg])
                P.op("vector", lambda e: e.tensor_add(lgc[:], lgc[:], br_b[:]),
                     reads=[klg, "br_b"], writes=[klg])
                P.op("vector", lambda e: e.reduce_max(mxc[:], lgc[:], axis=AX.X, negate=True),
                     reads=[klg], writes=[kmx])
                P.op("scalar", lambda e: e.activation(sec[:], lgc[:], AF.Exp,
                                                      bias=mxc[:], scale=1.0, accum_out=ssc[:]),
                     reads=[klg, kmx], writes=[kse, f"f_ssum{r2}"])
                P.op("vector", lambda e: e.reciprocal(gcc[:], ssc[:]),
                     reads=[f"f_ssum{r2}"], writes=[kg])
                P.op("vector", lambda e: e.tensor_scalar(mwc[:], zeros8[:], mxc[:], None,
                                                         op0=ALU.subtract),
                     reads=["zeros8", kmx], writes=[f"f_mw{r2}"])
                P.op("vector", lambda e, m=miuc: e.max_index(m[:], mwc[:], lgc[:]),
                     reads=[f"f_mw{r2}", klg], writes=[f"f_mi{r2}"])
                P.op("vector", lambda e, m=miuc: e.tensor_copy(ecc[:], m[:, 0:1]),
                     reads=[f"f_mi{r2}"], writes=[ke])
                P.op("vector", lambda e, i=i: e.tensor_scalar(oh_all[:, ts(i, 8)], iota8[:], ecc[:], None,
                                                              op0=ALU.is_equal),
                     reads=["iota8", ke], writes=[f"oh:{i}"])
                P.op("vector", lambda e, i=i: e.tensor_copy(gate_hi[:, i : i + 1], gcc[:]),
                     reads=[kg], writes=[f"gh:{i}"])
                P.op("vector", lambda e, i=i: e.tensor_copy(zgc[:, 1024:1025], gcc[:]),
                     reads=[kg], writes=[kzg], accum=True)
                P.op("vector", lambda e, i=i: e.tensor_tensor(glc[:], gcc[:], zgc[:, 1024:1025],
                                                              op=ALU.subtract),
                     reads=[kg, kzg], writes=[f"f_glo{r2}"])
                P.op("vector", lambda e: e.tensor_copy(zgc[:, 1025:1026], glc[:]),
                     reads=[f"f_glo{r2}"], writes=[kzg], accum=True)
                P.op("vector", lambda e: e.memset(zgc[:, 1026:ROWB], 0.0), writes=[kzg], accum=True)
                P.op("sync", lambda e, i=i, b=zgc: e.dma_start(out=xn2nat[ts(i, 128), :], in_=b[:]),
                     reads=[kzg], writes=[f"zn:{i}"], dma=True)

            for i in range(NT):
                f_tile(i)
            for i in range(NT):
                P.op("tensor", lambda e, i=i: e.matmul(pC[0:8, 0:1], oh_all[:, ts(i, 8)], onesc_f[:],
                                                       start=(i == 0), stop=(i == NT - 1)),
                     reads=[f"oh:{i}", "onesc_f"], writes=["pC0"], quiet=(i < NT - 1))
            P.op("vector", lambda e: e.tensor_copy(cnt8[:], pC[0:8, 0:1]), reads=["pC0"], writes=["cnt8"])
            tq = AR.alloc("f", 1)
            tqf = AR.alloc("f", 1)
            cof = AR.alloc("f", 1)
            P.op("vector", lambda e: e.tensor_scalar(tq[0:8, :], cnt8[:], 511.0, 1.0 / 512.0,
                                                     op0=ALU.add, op1=ALU.mult),
                 reads=["cnt8"], writes=["f_tq"])
            P.op("vector", lambda e: e.tensor_copy(i32b[:], tq[0:8, :]), reads=["f_tq"], writes=["f_tqi"])
            P.op("vector", lambda e: e.tensor_copy(tqf[0:8, :], i32b[:]), reads=["f_tqi"], writes=["f_tqf"])
            P.op("vector", lambda e: e.tensor_tensor(cof[0:8, :], tqf[0:8, :], tq[0:8, :], op=ALU.is_gt),
                 reads=["f_tqf", "f_tq"], writes=["f_co"])
            P.op("vector", lambda e: e.tensor_sub(tqf[0:8, :], tqf[0:8, :], cof[0:8, :]),
                 reads=["f_tqf", "f_co"], writes=["f_tqf"])
            P.op("tensor", lambda e: e.matmul(pC[0:8, 0:1], ltmask[:], tqf[0:8, :], start=True, stop=True),
                 reads=["ltmask", "f_tqf"], writes=["pC0"])
            P.op("vector", lambda e: e.tensor_copy(ptilex[:], pC[0:8, 0:1]), reads=["pC0"], writes=["ptilex"])
            P.op("vector", lambda e: e.tensor_scalar(padoffm1[:], ptilex[:], 512.0, -1.0,
                                                     op0=ALU.mult, op1=ALU.add),
                 reads=["ptilex"], writes=["padoffm1"])
            P.op("vector", lambda e: e.memset(carry8[:], 0.0), writes=["carry8"])
            for c in range(16):
                for k in range(4):
                    i = 4 * c + k
                    pb, pk = sub128("D", k)
                    P.op("tensor", lambda e, i=i, p=pb: e.transpose(p[0:8, :], oh_all[:, ts(i, 8)], ident[:]),
                         reads=[f"oh:{i}", "ident"], writes=[pk])
                    P.op("vector", lambda e, k=k, p=pb: e.tensor_copy(ohT[:, ts(k, 128)], p[0:8, :]),
                         reads=[pk], writes=["ohT"], accum=(k > 0))
                P.op("vector", lambda e: e.tensor_tensor_scan(scanbuf[:], ohT[:], scanone[:], carry8[:],
                                                              op0=ALU.add, op1=ALU.mult),
                     reads=["ohT", "scanone", "carry8"], writes=["scanbuf"])
                P.op("vector", lambda e: e.tensor_copy(carry8[:], scanbuf[:, 511:512]),
                     reads=["scanbuf"], writes=["carry8"])
                P.op("vector", lambda e: e.scalar_tensor_tensor(tmpT8[:], scanbuf[:], padoffm1[:], ohT[:],
                                                                op0=ALU.add, op1=ALU.mult),
                     reads=["scanbuf", "padoffm1", "ohT"], writes=["tmpT8"])
                P.op("tensor", lambda e: e.matmul(pC[0:1, 0:512], onesc_f[0:8, :], tmpT8[:],
                                                  start=True, stop=True),
                     reads=["tmpT8", "onesc_f"], writes=["pC0"])
                P.op("vector", lambda e: e.tensor_copy(drow[:], pC[0:1, 0:512]), reads=["pC0"], writes=["drow"])
                for k in range(4):
                    i = 4 * c + k
                    pb, pk = sub128("D", 4 + k)
                    P.op("tensor", lambda e, k=k, p=pb: e.transpose(p[:, 0:1], drow[:, ts(k, 128)], ident[0:1, 0:1]),
                         reads=["drow", "ident"], writes=[pk])
                    P.op("vector", lambda e, i=i, p=pb: e.tensor_copy(dest_i[:, i : i + 1], p[:, 0:1]),
                         reads=[pk], writes=[f"dst:{i}"])
            P.op("tensor", lambda e: e.matmul(pC[0:8, 0:NST], ones_f[0:1, 0:8], iota23[:],
                                              start=True, stop=True),
                 reads=["ones_f", "iota23"], writes=["pC0"])
            P.op("vector", lambda e: e.tensor_scalar(ecmp[:], pC[0:8, 0:NST], ptilex[:], None, op0=ALU.is_ge),
                 reads=["pC0", "ptilex"], writes=["ecmp"])
            P.op("tensor", lambda e: e.matmul(pC[0:1, 0:NST], onesc_f[0:8, :], ecmp[:], start=True, stop=True),
                 reads=["ecmp", "onesc_f"], writes=["pC0"])
            P.op("vector", lambda e: e.tensor_scalar_add(erow_m1[:], pC[0:1, 0:NST], -1.0),
                 reads=["pC0"], writes=["erow_m1"])
            ebc = AR.alloc("f", NST)
            idf = AR.alloc("f", NST)
            for (mul, nfc, idxt, kidx) in ((1024.0, 8, idxw1, "idxw1"), (512.0, 4, idxw2, "idxw2")):
                P.op("vector", lambda e, m=mul: e.tensor_scalar_mul(drow[:, 0:NST], erow_m1[:], m),
                     reads=["erow_m1"], writes=["drow"])
                P.op("tensor", lambda e: e.matmul(pC[:, 0:NST], ones_f[:], drow[:, 0:NST],
                                                  start=True, stop=True),
                     reads=["drow", "ones_f"], writes=["pC0"])
                P.op("vector", lambda e: e.tensor_copy(ebc[:], pC[:, 0:NST]), reads=["pC0"], writes=["f_ebc"])
                for fc in range(nfc):
                    P.op("vector", lambda e, fc=fc: e.tensor_scalar(
                        idf[:], ebc[:], iota_p[:], float(fc * 128), op0=ALU.add, op1=ALU.add),
                         reads=["f_ebc", "iota_p"], writes=["f_idf"])
                    P.op("vector", lambda e, fc=fc, t=idxt: e.tensor_copy(
                        t[:, fc * NST : (fc + 1) * NST], idf[:]),
                         reads=["f_idf"], writes=[kidx], accum=(fc > 0))
            P.op("tensor", lambda e: e.matmul(pC[0:2, 0:NST], ones_f[0:1, 0:2], erow_m1[:],
                                              start=True, stop=True),
                 reads=["ones_f", "erow_m1"], writes=["pC0"])
            P.op("vector", lambda e: e.tensor_copy(idxbr[:], pC[0:2, 0:NST]), reads=["pC0"], writes=["idxbr"])
            P.fence()
            AR.pop()
            AR.push()
            sz = [AR.alloc("b", ROWB) for _ in range(2)]
            for i in range(NT):
                r2 = i % 2
                P.op("sync", lambda e, i=i, b=sz[r2]: e.dma_start(out=b[:], in_=xn2nat[ts(i, 128), :]),
                     reads=[f"zn:{i}"], writes=[f"f_sz{r2}"], dma=True)
                P.op("gpsimd", lambda e, i=i, b=sz[r2]: e.indirect_dma_start(
                    xn2srt[:, :], IndirectOffsetOnAxis(dest_i[:, i : i + 1], 0), b[:], None),
                     reads=[f"f_sz{r2}", f"dst:{i}"], writes=["srt"], dma=True, accum=True)
            AR.pop()
            P.fence()
            AR.push()
            gt = [AR.alloc("b", 4 * ROWB) for _ in range(2)]
            gtf = [AR.alloc("f", 1024) for _ in range(2)]
            xgT = [AR.alloc("b", 4096)]
            w1b = [AR.alloc("b", 4096) for _ in range(2)]
            w2b = [AR.alloc("b", 8192) for _ in range(2)]
            hT = AR.alloc("b", 16384)
            yb = AR.alloc("b", 4096)
            br1 = [AR.alloc("b", 4096)]
            br2 = [AR.alloc("b", 1024)]
            gates = AR.alloc("f", 4)
            for st in range(NST):
                r2 = st % 2
                kgt, kxgT = f"f_gt{r2}", "f_xgT"
                P.op("sync", lambda e, st=st, b=gt[r2]: e.dma_start(
                    out=b[:], in_=xn2srt.rearrange("(t s p) c -> t p s c", t=TPAD // 512, s=4, p=128)[st]),
                     reads=["srt"], writes=[kgt], dma=True)
                P.op("gpsimd", lambda e, st=st, b=br1[0]: e.indirect_dma_start(
                    b[0:2, :], None, b1p_d[:, :], IndirectOffsetOnAxis(idxbr[:, st : st + 1], 0)),
                     reads=["b1p_dd", "idxbr"], writes=["f_br1"], dma=True)
                P.op("gpsimd", lambda e, st=st, b=br2[0]: e.indirect_dma_start(
                    b[0:2, :], None, b2p_d[:, :], IndirectOffsetOnAxis(idxbr[:, st : st + 1], 0)),
                     reads=["b2p_dd", "idxbr"], writes=["f_br2"], dma=True)
                for s in range(4):
                    P.op("vector", lambda e, s=s, b=gt[r2]: e.tensor_tensor(
                        gates[:, s : s + 1], b[:, s * ROWB + 1024 : s * ROWB + 1025],
                        b[:, s * ROWB + 1025 : s * ROWB + 1026], op=ALU.add),
                         reads=[kgt], writes=["f_gates"], accum=(s > 0))
                for s in range(4):
                    kf = f"f_gtf{s % 2}"
                    P.op("vector", lambda e, s=s, b=gt[r2]: e.tensor_copy(
                        gtf[s % 2][:], b[:, s * ROWB : s * ROWB + 1024]),
                         reads=[kgt], writes=[kf])
                    for j in range(8):
                        pb, pk = sub128("D", j)
                        P.op("tensor", lambda e, s=s, j=j, p=pb: e.transpose(
                            p[:], gtf[s % 2][:, ts(j, 128)], ident[:]),
                             reads=[kf, "ident"], writes=[pk])
                        P.op("vector", lambda e, s=s, j=j, p=pb: e.tensor_scalar(
                            xgT[0][:, j * 512 + s * 128 : j * 512 + (s + 1) * 128], p[:],
                            g_cols[:, j : j + 1], bt_colsf[:, j : j + 1],
                            op0=ALU.mult, op1=ALU.add),
                             reads=[pk, "g_cols", "bt_colsf"], writes=[kxgT], accum=(s + j > 0))
                for fc in range(8):
                    rw = (st * 8 + fc) % 2
                    P.op("gpsimd", lambda e, st=st, fc=fc, b=w1b[rw]: e.indirect_dma_start(
                        b[:], None, w1p[:, :],
                        IndirectOffsetOnAxis(idxw1[:, fc * NST + st : fc * NST + st + 1], 0)),
                         reads=["w1p_d", "idxw1"], writes=[f"f_w1b{rw}"], dma=True)
                    for fs in range(4):
                        pb, pk = bank("AB"[fs // 2], fs % 2)
                        for j in range(8):
                            P.op("tensor", lambda e, fc=fc, fs=fs, j=j, p=pb, w=w1b[rw]: e.matmul(
                                p[:], w[:, j * 512 + fs * 128 : j * 512 + (fs + 1) * 128],
                                xgT[0][:, ts(j, 512)], start=(j == 0), stop=False),
                                 reads=[f"f_w1b{rw}", kxgT], writes=[pk], quiet=True)
                        P.op("tensor", lambda e, fc=fc, fs=fs, p=pb, b=br1[0]: e.matmul(
                            p[:], b[0:1, fc * 512 + fs * 128 : fc * 512 + (fs + 1) * 128],
                            ones_r512[:], start=False, stop=True),
                             reads=["f_br1", "ones_r512"], writes=[pk])
                        P.op("scalar", lambda e, fc=fc, fs=fs, p=pb: e.activation(
                            hT[:, (fc * 4 + fs) * 512 : (fc * 4 + fs + 1) * 512], p[:],
                            AF.Relu, bias=0.0, scale=1.0),
                             reads=[pk], writes=["f_hT"], accum=(fc + fs > 0))
                for dc in range(4):
                    rw2 = (st * 4 + dc) % 2
                    P.op("gpsimd", lambda e, st=st, dc=dc, b=w2b[rw2]: e.indirect_dma_start(
                        b[:], None, w2p[:, :],
                        IndirectOffsetOnAxis(idxw2[:, dc * NST + st : dc * NST + st + 1], 0)),
                         reads=["w2p_d", "idxw2"], writes=[f"f_w2b{rw2}"], dma=True)
                    for s in range(4):
                        pbf, pk = bank("AB"[s // 2], s % 2)
                        pb = pbf[:, 0:256]
                        for j in range(32):
                            P.op("tensor", lambda e, dc=dc, s=s, j=j, p=pb, w=w2b[rw2]: e.matmul(
                                p[:], hT[:, j * 512 + s * 128 : j * 512 + (s + 1) * 128],
                                w[:, ts(j, 256)], start=(j == 0), stop=False),
                                 reads=[f"f_w2b{rw2}", "f_hT"], writes=[pk], quiet=True)
                        P.op("tensor", lambda e, dc=dc, s=s, p=pb, b=br2[0]: e.matmul(
                            p[:], ones_b[:], b[0:1, ts(dc, 256)], start=False, stop=True),
                             reads=["f_br2", "ones_b"], writes=[pk])
                        P.op("vector", lambda e, dc=dc, s=s, p=pb: e.tensor_scalar(
                            yb[:, s * 1024 + dc * 256 : s * 1024 + (dc + 1) * 256], p[:],
                            gates[:, s : s + 1], None, op0=ALU.mult),
                             reads=[pk, "f_gates"], writes=["f_yb"], accum=(dc + s > 0))
                P.op("sync", lambda e, st=st: e.dma_start(
                    out=ysrt.rearrange("(t s p) c -> t p s c", t=TPAD // 512, s=4, p=128)[st],
                    in_=yb[:]),
                     reads=["f_yb"], writes=["ysrt_d"], dma=True, accum=True)
            AR.pop()
            P.fence()
            AR.push()
            yt = [AR.alloc("b", 1024) for _ in range(2)]
            rl5 = [AR.alloc("f", 1024) for _ in range(2)]
            ro5 = [AR.alloc("f", 1024) for _ in range(2)]
            for i in range(NT):
                r2 = i % 2
                P.op("gpsimd", lambda e, i=i, b=yt[r2]: e.indirect_dma_start(
                    b[:], None, ysrt[0 : NST * 512, :], IndirectOffsetOnAxis(dest_i[:, i : i + 1], 0)),
                     reads=["ysrt_d", f"dst:{i}"], writes=[f"f_yt{r2}"], dma=True)
                P.op("sync", lambda e, i=i, b=rl5[r2]: e.dma_start(out=b[:], in_=Rin[ts(i, 128), :]),
                     reads=[f"{kin}:{i}"] if kin else [], writes=[f"f_rl5{r2}"], dma=True)
                P.op("vector", lambda e, b=yt[r2], o=ro5[r2], a=rl5[r2]: e.tensor_add(o[:], a[:], b[:]),
                     reads=[f"f_yt{r2}", f"f_rl5{r2}"], writes=[f"f_ro5{r2}"])
                P.op("sync", lambda e, i=i, b=ro5[r2]: e.dma_start(out=Rout[ts(i, 128), :], in_=b[:]),
                     reads=[f"f_ro5{r2}"], writes=[f"{kout}:{i}"], dma=True)
            AR.pop()
            AR.pop()
            P.fence()

        def attn_b(Rin, Rout, kin, kout):
            AR.push()
            xnbT = AR.alloc("b", 16384)
            rrow = [AR.alloc("b", 512)[0:1, :] for _ in range(2)]
            rrow_f = [AR.alloc("f", 512)[0:1, :] for _ in range(2)]
            esb = AR.alloc("b", 512)
            wvo_sb = AR.alloc("b", 8192)
            wqk = [AR.alloc("b", 2048)]
            wqk = [wqk[0], wqk[0]]
            qrT = [AR.alloc("b", 2048) for _ in range(2)]
            krT = [AR.alloc("b", 2048) for _ in range(2)]
            q0c = [AR.alloc("b", 512) for _ in range(2)]
            rtmp = [AR.alloc("b", 512)]
            rtmp = [rtmp[0], rtmp[0]]
            probs = [AR.alloc("b", 512) for _ in range(4)]
            vsl = [AR.alloc("b", 130) for _ in range(3)]
            attnT = AR.alloc("b", 16384)
            rl = [AR.alloc("f", 1024) for _ in range(3)]
            xn = [AR.alloc("f", 1024) for _ in range(2)]
            vtile = [AR.alloc("b", 1040) for _ in range(2)]
            ro = [AR.alloc("f", 1024) for _ in range(2)]
            for rv_ in range(2):
                P.op("vector", lambda e, b=vtile[rv_]: e.memset(
                    b[:].rearrange("p (hp g c) -> p hp g c", hp=8, g=2, c=65)[:, :, :, 64:65], 1.0),
                     writes=[f"b_vt{rv_}"])
            for s_ in range(NSEQ):
                sbase = s_ * 16
                for t in range(16):
                    i = sbase + t
                    r3, r2 = t % 3, t % 2
                    krl, kxn = f"b_rl{r3}", f"b_xn{r2}"
                    P.op("sync", lambda e, i=i, b=rl[r3]: e.dma_start(out=b[:], in_=Rin[ts(i, 128), :]),
                         reads=[f"{kin}:{i}"] if kin else [], writes=[krl], dma=True)
                    emit_ln(rl[r3][:], krl, xn[r2][:], kxn, t % 4)
                    for j in range(8):
                        pb, pk = sub128("D", j)
                        P.op("tensor", lambda e, j=j, p=pb, x=xn[r2]: e.transpose(p[:], x[:, ts(j, 128)], ident[:]),
                             reads=[kxn, "ident"], writes=[pk])
                        P.op("scalar", lambda e, j=j, t=t, p=pb: e.copy(
                            xnbT[:, j * 2048 + t * 128 : j * 2048 + (t + 1) * 128], p[:]),
                             reads=[pk], writes=["b_xnbT"], accum=(t + j > 0))
                P.op("gpsimd", lambda e: e.dma_start(
                    out=wvo_sb, in_=wvT_d.rearrange("(j p) c -> p j c", j=8, p=128)),
                     reads=["b_wvo"] if s_ > 0 else [], writes=["b_wvo"], dma=True)
                for t in range(16):
                    r2 = t % 2
                    for c2 in range(2):
                        pb, pk = bank("C", c2)
                        for j in range(8):
                            P.op("tensor", lambda e, t=t, c2=c2, j=j, p=pb: e.matmul(
                                p[:], xnbT[:, j * 2048 + t * 128 : j * 2048 + (t + 1) * 128],
                                wvo_sb[:, j * 1024 + c2 * 512 : j * 1024 + (c2 + 1) * 512],
                                start=(j == 0), stop=(j == 7)),
                                 reads=["b_xnbT", "b_wvo"], writes=[pk], quiet=(j < 7))
                        P.op("scalar", lambda e, c2=c2, p=pb, b=vtile[r2]: e.copy(
                            b[:].rearrange("p (hp g c) -> p hp g c", hp=8, g=2, c=65)[:, 4 * c2 : 4 * c2 + 4, :, 0:64],
                            p[:].rearrange("p (a g c) -> p a g c", a=4, g=2, c=64)),
                             reads=[pk], writes=[f"b_vt{r2}"], accum=True)
                    P.op("sync", lambda e, s_=s_, t=t, b=vtile[r2]: e.dma_start(
                        out=v_d.rearrange("(s hp t p) c -> s t p hp c", s=NSEQ, hp=NHP, t=16, p=128)[s_, t],
                        in_=b[:].rearrange("p (hp c) -> p hp c", hp=8, c=130)),
                         reads=[f"b_vt{r2}"], writes=["v_dd"], dma=True, accum=True)
                if DEBUG and s_ == 0:
                    P.op("sync", lambda e: e.dma_start(out=dbg["d_xnbT"][:, :], in_=xnbT[:, 0:2048]),
                         reads=["b_xnbT"], writes=["dbgxnbT"], dma=True)
                    P.op("sync", lambda e: e.dma_start(out=dbg["d_vt"][:, :], in_=v_d[0:2048, :]),
                         reads=["v_dd"], writes=["dbgvt"], dma=True)
                    P.op("sync", lambda e: e.dma_start(out=dbg["d_ropeC"][:, :], in_=ropeC[:]),
                         reads=["ropeC"], writes=["dbgrc"], dma=True)
                    P.op("sync", lambda e: e.dma_start(out=dbg["d_ropeS"][:, :], in_=ropeS[:]),
                         reads=["ropeS"], writes=["dbgrs"], dma=True)
                def hp_body(hp):
                    rh = hp % 2
                    kwqk, kqr, kkr = "b_wqk0", f"b_qr{rh}", f"b_kr{rh}"
                    wqkc, qrc, krc = wqk[rh], qrT[rh], krT[rh]
                    P.op("gpsimd", lambda e, hp=hp, b=wqkc: e.dma_start(
                        out=b[:, 0:1024],
                        in_=wqT_d.rearrange("(j p) c -> p j c", j=8, p=128)[:, :, ts(hp, 128)]),
                         writes=[kwqk], dma=True)
                    P.op("gpsimd", lambda e, hp=hp, b=wqkc: e.dma_start(
                        out=b[:, 1024:2048],
                        in_=wkT_d.rearrange("(j p) c -> p j c", j=8, p=128)[:, :, ts(hp, 128)]),
                         writes=[kwqk], dma=True, accum=True)
                    for side, dstb, kdst in ((0, qrc, kqr), (1, krc, kkr)):
                        for c in range(4):
                            pb, pk = bank("A", c % 2)
                            for j in range(8):
                                P.op("tensor", lambda e, side=side, c=c, j=j, p=pb, w=wqkc: e.matmul(
                                    p[:], w[:, side * 1024 + j * 128 : side * 1024 + (j + 1) * 128],
                                    xnbT[:, j * 2048 + c * 512 : j * 2048 + (c + 1) * 512],
                                    start=(j == 0), stop=(j == 7)),
                                     reads=[kwqk, "b_xnbT"], writes=[pk], quiet=(j < 7))
                            rq = c % 2
                            if side == 0:
                                P.op("scalar", lambda e, p=pb, b=q0c[rq]: e.mul(b[:], p[:], 0.125),
                                     reads=[pk], writes=[f"b_q0{rq}"])
                            else:
                                P.op("scalar", lambda e, p=pb, b=q0c[rq]: e.copy(b[:], p[:]),
                                     reads=[pk], writes=[f"b_q0{rq}"])
                            pb2, pk2 = bank("B", c % 2)
                            P.op("tensor", lambda e, p=pb2, b=q0c[rq]: e.matmul(
                                p[:], rPT[:], b[:], start=True, stop=True),
                                 reads=["rPT", f"b_q0{rq}"], writes=[pk2])
                            P.op("vector", lambda e, c=c, p=pb2, b=rtmp[rq]: e.tensor_tensor(
                                b[:], p[:], ropeS[:, ts(c, 512)], op=ALU.mult),
                                 reads=[pk2, "ropeS"], writes=["b_rt0"])
                            P.op("vector", lambda e, c=c, d=dstb, b=q0c[rq]: e.tensor_tensor(
                                d[:, ts(c, 512)], b[:], ropeC[:, ts(c, 512)], op=ALU.mult),
                                 reads=[f"b_q0{rq}", "ropeC"], writes=[kdst], accum=(c > 0))
                            P.op("vector", lambda e, c=c, d=dstb, b=rtmp[rq]: e.tensor_add(
                                d[:, ts(c, 512)], d[:, ts(c, 512)], b[:]),
                                 reads=[kdst, "b_rt0"], writes=[kdst], accum=True)
                    HB = (0, 64)
                    for qc in range(4):
                        pb2, pk2 = bank("D", 0)
                        nkt = 4 * qc + 4

                        def emit_vdma(kt):
                            rv = kt % 3
                            P.op("sync", lambda e, s_=s_, hp=hp, kt=kt, b=vsl[rv]: e.dma_start(
                                out=b[:], in_=v_d[(s_ * NHP + hp) * SEQ + kt * 128 :
                                                  (s_ * NHP + hp) * SEQ + (kt + 1) * 128, :]),
                                 reads=["v_dd"], writes=[f"b_vs{rv}"], dma=True)

                        def emit_st(kt):
                            for h in (0, 1):
                                pb, pk = bank("AC"[h], kt % 2)
                                P.op("tensor", lambda e, qc=qc, kt=kt, hb=HB[h], p=pb: e.matmul(
                                    p[:], krc[hb : hb + 64, ts(kt, 128)],
                                    qrc[hb : hb + 64, ts(qc, 512)],
                                    start=True, stop=True),
                                     reads=[kkr, kqr], writes=[pk])
                        emit_vdma(0)
                        emit_st(0)
                        for kt in range(nkt):
                            rv = kt % 3
                            if kt + 1 < nkt:
                                emit_vdma(kt + 1)
                                emit_st(kt + 1)
                            for h in (0, 1):
                                pb, pk = bank("AC"[h], kt % 2)
                                kpr = f"b_pr{2 * (kt % 2) + h}"
                                P.op("scalar", lambda e, p=pb, b=probs[2 * (kt % 2) + h]: e.activation(
                                    b[:], p[:], AF.Exp, bias=0.0, scale=1.0),
                                     reads=[pk], writes=[kpr])
                            if kt >= 4 * qc:
                                for h in (0, 1):
                                    kpr = f"b_pr{2 * (kt % 2) + h}"
                                    P.op("gpsimd", lambda e, qc=qc, kt=kt, b=probs[2 * (kt % 2) + h]: e.affine_select(
                                        b[:], b[:], pattern=[[1, 512]], compare_op=ALU.is_ge,
                                        fill=0.0, base=qc * 512 - kt * 128, channel_multiplier=-1),
                                         reads=[kpr], writes=[kpr])
                            for h in (0, 1):
                                kpr = f"b_pr{2 * (kt % 2) + h}"
                                P.op("tensor", lambda e, h=h, kt=kt, qc=qc, b=vsl[rv],
                                     pr=probs[2 * (kt % 2) + h]: e.matmul(
                                    pB[0:65, h * 512 : (h + 1) * 512], b[:, h * 65 : h * 65 + 65], pr[:],
                                    start=(kt == 0), stop=(kt == 4 * qc + 3)),
                                     reads=[f"b_vs{rv}", kpr], writes=[f"pB{h}"], accum=(kt > 0),
                                     quiet=(kt < 4 * qc + 3))
                        for h in (0, 1):
                            P.op("vector", lambda e, h=h: e.reciprocal(
                                rrow_f[h][:], pB[64:65, h * 512 : (h + 1) * 512]),
                                 reads=[f"pB{h}"], writes=[f"b_rf{h}"])
                            P.op("vector", lambda e, h=h: e.tensor_copy(rrow[h][:], rrow_f[h][:]),
                                 reads=[f"b_rf{h}"], writes=[f"b_rr{h}"])
                        for h in (0, 1):
                            P.op("tensor", lambda e, h=h, p=pb2: e.matmul(
                                p[h * 64 : h * 64 + 64, :], ones_b[0:1, 0:64], rrow[h][:],
                                start=True, stop=True, tile_position=(0, h * 64)),
                                 reads=["ones_b", f"b_rr{h}"], writes=[pk2], accum=(h > 0))
                        P.op("scalar", lambda e, p=pb2: e.copy(esb[:], p[:]),
                             reads=[pk2], writes=["b_esb"])
                        for h in (0, 1):
                            P.op("vector", lambda e, h=h, hp=hp, qc=qc: e.tensor_mul(
                                attnT[h * 64 : (h + 1) * 64,
                                      hp * 2048 + qc * 512 : hp * 2048 + (qc + 1) * 512],
                                pB[0:64, h * 512 : (h + 1) * 512], esb[h * 64 : (h + 1) * 64, :]),
                                 reads=[f"pB{h}", "b_esb"], writes=["b_attnT"],
                                 accum=(hp + qc + h > 0))
                for hp in range(NHP):
                    hp_body(hp)
                P.op("gpsimd", lambda e: e.dma_start(
                    out=wvo_sb, in_=woT_d.rearrange("(j p) c -> p j c", j=8, p=128)),
                     reads=["b_wvo"], writes=["b_wvo"], dma=True)
                for t in range(16):
                    i = sbase + t
                    r3, r2 = t % 3, t % 2
                    P.op("sync", lambda e, i=i, b=rl[r3]: e.dma_start(out=b[:], in_=Rin[ts(i, 128), :]),
                         reads=[f"{kin}:{i}"] if kin else [], writes=[f"b_rl{r3}"], dma=True)
                    for c2 in range(2):
                        pb, pk = bank("C", c2)
                        for hp in range(8):
                            P.op("tensor", lambda e, t=t, c2=c2, hp=hp, p=pb: e.matmul(
                                p[:], attnT[:, hp * 2048 + t * 128 : hp * 2048 + (t + 1) * 128],
                                wvo_sb[:, hp * 1024 + c2 * 512 : hp * 1024 + (c2 + 1) * 512],
                                start=(hp == 0), stop=(hp == 7)),
                                 reads=["b_attnT", "b_wvo"], writes=[pk], quiet=(hp < 7))
                        P.op("vector", lambda e, c2=c2, p=pb, b=ro[r2], a=rl[r3]: e.tensor_add(
                            b[:, ts(c2, 512)], p[:], a[:, ts(c2, 512)]),
                             reads=[pk, f"b_rl{r3}"], writes=[f"b_ro{r2}"], accum=(c2 > 0))
                    P.op("sync", lambda e, i=i, b=ro[r2]: e.dma_start(out=Rout[ts(i, 128), :], in_=b[:]),
                         reads=[f"b_ro{r2}"], writes=[f"{kout}:{i}"], dma=True)
            AR.pop()
            P.fence()

        AR.push()
        init_consts()
        P.fence()
        AR.pop()
        if n_sub >= 2:
            zb = AR.alloc("b", ROWB)
            P.op("vector", lambda e: e.memset(zb[:], 0.0), writes=["zb"])
            for r in range(TPAD // 128):
                P.op("sync", lambda e, r=r: e.dma_start(out=xn2srt[ts(r, 128), :], in_=zb[:]),
                     reads=["zb"], writes=["srt"], dma=True, accum=True)
        keys = [None, "Rd1", "Rd2", "Rd3", "Rd4"]
        AR.push()
        rcg = make_recast_gen() if n_sub >= 2 else None
        attn_a(RS[0], RS[1], keys[0], keys[1], bg=rcg)
        if rcg is not None:
            for _ in rcg:
                pass
            P.fence()
        AR.pop()
        if n_sub >= 2:
            ffn(1, RS[1], RS[2], keys[1], keys[2])
        if n_sub >= 3:
            attn_b(RS[2], RS[3], keys[2], keys[3])
        if n_sub >= 4:
            ffn(2, RS[3], RS[4], keys[3], keys[4])

        def mk(engname):
            def _go(eng):
                for waits, fn, sem, inc in P.plan[engname]:
                    for s, v in waits:
                        eng.wait_ge(semh[s], v)
                    inst = fn(eng)
                    if inst is not None and sem is not None:
                        inst.then_inc(semh[sem], inc)
                if engname == "sync":
                    for s in ALLSEMS:
                        if P.counts[s] > 0:
                            eng.wait_ge(semh[s], P.counts[s])
            return _go

        block.sync(mk("sync"))
        block.tensor(mk("tensor"))
        block.vector(mk("vector"))
        block.scalar(mk("scalar"))
        block.gpsimd(mk("gpsimd"))
    return nc


_CACHE = {}


def kernel(**inputs):
    n_sub = SUBLAYERS
    if n_sub not in _CACHE:
        _CACHE[n_sub] = (build_program(n_sub), host_consts())
    nc, hc = _CACHE[n_sub]
    q = np.ascontiguousarray(inputs["L0_attn_a_query"], dtype=np.float32)
    wqkv = np.asarray(inputs["L0_attn_a_W_qkv"])
    com = {
        "wqvT_d": np.ascontiguousarray(
            np.concatenate([wqkv[0:D].T, wqkv[2 * D : 3 * D].T], axis=1)),
        "b_qkv": np.ascontiguousarray(inputs["L0_attn_a_b_qkv"]).reshape(1, 3 * D),
        "waoT_d": np.ascontiguousarray(np.asarray(inputs["L0_attn_a_W_out"]).T),
        "b_ao": np.ascontiguousarray(inputs["L0_attn_a_b_out"]).reshape(1, D),
        "wq_a": np.ascontiguousarray(inputs["L0_attn_a_weight_q"]),
        "wv_a": np.ascontiguousarray(inputs["L0_attn_a_weight_v"]),
        "wqT_d": np.ascontiguousarray(np.asarray(inputs["L0_attn_b_Wq"]).T),
        "wkT_d": np.ascontiguousarray(np.asarray(inputs["L0_attn_b_Wk"]).T),
        "wvT_d": np.ascontiguousarray(np.asarray(inputs["L0_attn_b_Wv"]).T),
        "woT_d": np.ascontiguousarray(np.asarray(inputs["L0_attn_b_Wo"]).T),
        "w_r": np.ascontiguousarray(inputs["L0_ffn_Wr"]),
        "b_r": np.ascontiguousarray(inputs["L0_ffn_br"]).reshape(1, E),
        "ln_g": np.ascontiguousarray(inputs["L0_ffn_ln_g"]).reshape(1, D),
        "ln_b": np.ascontiguousarray(inputs["L0_ffn_ln_b"]).reshape(1, D),
        "w1f": np.ascontiguousarray(
            np.asarray(inputs["L0_ffn_W1"]).reshape(E, 8, 128, 8, 512)
            .transpose(0, 3, 2, 1, 4).reshape(E * 8 * 128, FF)),
        "b1": np.ascontiguousarray(inputs["L0_ffn_b1"]),
        "w2f": np.ascontiguousarray(
            np.asarray(inputs["L0_ffn_W2"]).reshape(E, 32, 128, 4, 256)
            .transpose(0, 3, 2, 1, 4).reshape(E * 4 * 128, 32 * 256)),
        "b2": np.ascontiguousarray(inputs["L0_ffn_b2"]),
    }
    com.update(hc)
    in_maps = []
    for c in range(8):
        m = dict(com)
        m["x_in"] = np.ascontiguousarray(q[c * NSEQ : (c + 1) * NSEQ].reshape(T, D))
        in_maps.append(m)
    res = run_bass_kernel_spmd(nc, in_maps, core_ids=list(range(8)))
    out = np.concatenate(
        [np.asarray(r["x_out"], dtype=np.float32).reshape(NSEQ, SEQ, D) for r in res.results], axis=0)
    return out
